# version 02
# baseline (speedup 1.0000x reference)
import math
from contextlib import ExitStack

import numpy as np
import ml_dtypes

import concourse.bass as bass
import concourse.mybir as mybir
from concourse.bass_utils import run_bass_kernel_spmd

F32 = mybir.dt.float32
BF16 = mybir.dt.bfloat16
U8 = mybir.dt.uint8
U32 = mybir.dt.uint32
AF = mybir.ActivationFunctionType
ALU = mybir.AluOpType
BF = ml_dtypes.bfloat16

B, N, D = 4, 8192, 2048
H, DH, C = 16, 128, 128
HID = 8192
NCORE = 8
TC = 4096
NCH = 32
G = 4
NG = NCH // G
GT = G * C
NDI = D // 128
NHH = HID // 128
BASE = 10000.0

LOG_LAM = [-(2.0 ** (-(h + 1) / 2.0)) for h in range(H)]
LAM_C = [math.exp(l * C) for l in LOG_LAM]
FWHT_SCALE = 1.0 / math.sqrt(HID)

ENGS = ("sync", "scalar", "vector", "tensor", "gpsimd")

DEBUG_TAPS = ()


class Prog:

    def __init__(self, nc):
        self.nc = nc
        self.q = {e: [] for e in ENGS}
        self.sems = {}
        self.cnt = {}

    def sem(self, name):
        s = self.nc.alloc_semaphore(name)
        self.sems[name] = s
        self.cnt[name] = 0
        return s

    AUTO = {"vector": "dv", "scalar": "ac"}

    def emit(self, eng, fn, waits=(), inc=None, amt=1):
        if fn is not None and eng in self.AUTO and inc is None:
            inc = self.AUTO[eng]
            amt = 1
        self.q[eng].append((fn, tuple(waits), inc, amt))
        if inc is not None:
            self.cnt[inc] += amt
            return self.cnt[inc]
        return None

    def wait(self, eng, waits):
        self.q[eng].append((None, tuple(waits), None, 0))

    def val(self, name):
        return self.cnt[name]

    def barrier(self):
        self.cnt.setdefault("bar", 0)
        if "bar" not in self.sems:
            self.sems["bar"] = self.nc.alloc_semaphore("bar")
        self.cnt["bar"] += len(ENGS)
        tgt = self.cnt["bar"]
        for e in ENGS:
            self.q[e].append(("BARRIER", tgt, None, 0))

    def run(self, block):
        def mk(ename):
            auto = self.AUTO.get(ename)

            def body(eng):
                done = 0
                for fn, waits, inc, amt in self.q[ename]:
                    if fn == "BARRIER":
                        eng.drain()
                        eng.sem_inc(self.sems["bar"], 1)
                        eng.wait_ge(self.sems["bar"], waits)
                        continue
                    for (s, v) in waits:
                        if v > 0 and not (s == auto and v <= done):
                            eng.wait_ge(self.sems[s], v)
                    if fn is None:
                        continue
                    if auto is not None and done > 0:
                        eng.wait_ge(self.sems[auto], done)
                    inst = fn(eng)
                    if inc is not None:
                        inst.then_inc(self.sems[inc], amt)
                        if inc == auto:
                            done += amt
            return body

        block.sync(mk("sync"))
        block.scalar(mk("scalar"))
        block.vector(mk("vector"))
        block.tensor(mk("tensor"))
        block.gpsimd(mk("gpsimd"))


def host_constants(core):
    tok0 = (core % 2) * TC
    i = np.arange(64, dtype=np.float64)
    theta = BASE ** (-2.0 * i / 128.0)
    pos = tok0 + np.arange(TC, dtype=np.float64)
    ang = pos[:, None] * theta[None, :]
    cos = np.cos(ang)
    sin = np.sin(ang)
    tl = np.arange(TC) % C
    lam = np.asarray(LOG_LAM, dtype=np.float64)
    aq = np.exp(lam[:, None] * (tl[None, :] + 1.0 - 64.0))
    bk = np.exp(lam[:, None] * (63.0 - tl[None, :]))

    def table(trig, dec):
        t = trig.T[None, :, :] * dec[:, None, :]
        t = t.reshape(H, 64, NG, GT).transpose(2, 0, 1, 3)
        return np.ascontiguousarray(t, dtype=np.float32)

    s_i = np.arange(C)[:, None]
    t_i = np.arange(C)[None, :]
    m = (t_i >= s_i).astype(np.uint8)
    return {
        "cosq": table(cos, aq), "sinq": table(sin, aq),
        "cosk": table(cos, bk), "sink": table(sin, bk),
        "maskT": np.ascontiguousarray(np.tile(m, (1, H))),
        "id_bf": np.eye(128, dtype=np.float32).astype(BF),
        "id_f32": np.eye(128, dtype=np.float32),
        "ones_c": np.ones((128, 1), dtype=np.float32).astype(BF),
        "ones_r": np.ones((1, 128), dtype=np.float32),
        "cb": np.array([[1e-5, 1e-6]] * 128, dtype=np.float32),
        "sgsel": np.array([[core - 1 if core % 2 == 1 else 8]], dtype=np.uint32),
    }


def build_nc(taps=()):
    nc = bass.Bass("TRN2", target_bir_lowering=False, debug=False)

    def din(name, shape, dt=F32):
        return nc.dram_tensor(name, list(shape), dt, kind="ExternalInput").ap()

    x_in = din("x_core", (TC, D))
    wq_in = din("Wq", (D, D)); wk_in = din("Wk", (D, D)); wv_in = din("Wv", (D, D))
    wo_in = din("Wo", (D, D))
    wg1_in = din("Wg1", (D, 128)); wg2_in = din("Wg2", (128, D))
    nw_in = din("nw", (1, D))
    w1_in = din("W1", (HID, D)); w3_in = din("W3", (HID, D))
    w2s_in = din("W2s", (D // NCORE, HID))
    cosq_in = din("cosq", (NG, H, 64, GT)); sinq_in = din("sinq", (NG, H, 64, GT))
    cosk_in = din("cosk", (NG, H, 64, GT)); sink_in = din("sink", (NG, H, 64, GT))
    mask_in = din("maskT", (C, H * C), U8)
    idbf_in = din("id_bf", (128, 128), BF16)
    idf_in = din("id_f32", (128, 128))
    onc_in = din("ones_c", (128, 1), BF16)
    onr_in = din("ones_r", (1, 128))
    cb_in = din("cb", (128, 2))
    sgsel_in = din("sgsel", (1, 1), U32)

    y_out = nc.dram_tensor("y_core", [TC, D], F32, kind="ExternalOutput").ap()

    def dint(name, shape, dt):
        return nc.dram_tensor(name, list(shape), dt, kind="Internal").ap()

    wqb = dint("wqb", (D, D), BF16); wkb = dint("wkb", (D, D), BF16)
    wvb = dint("wvb", (D, D), BF16); wob = dint("wob", (D, D), BF16)
    wg1b = dint("wg1b", (D, 128), BF16); wg2b = dint("wg2b", (128, D), BF16)
    w1b = dint("w1b", (HID, D), BF16); w3b = dint("w3b", (HID, D), BF16)
    w1T = dint("w1T", (NHH, 128, NDI, 128), BF16)
    w3T = dint("w3T", (NHH, 128, NDI, 128), BF16)
    w2pb = dint("w2pb", (D // NCORE, HID), BF16)
    GW2 = dint("GW2", (D, HID), BF16)
    GT2 = dint("GT2", (NDI, 128, NHH, 128), BF16)
    Sout = dint("Sout", (H * DH, DH), F32)
    GS = dint("GS", (9 * H * DH, DH), F32)
    xn_scr = dint("xn_scr", (TC, D), BF16)
    sc_dbg = dint("sc_dbg", (NCH, 128, H * C), BF16)
    qt_scr = dint("qt_scr", (NG, 128, H, GT), BF16)
    o_scr = dint("o_scr", (NCH, 128, H * C), F32)
    g_scr = dint("g_scr", (NG, NDI, 128, GT), F32)

    es = ExitStack()
    sb = lambda n, shape, dt: es.enter_context(nc.sbuf_tensor(n, list(shape), dt))
    mask_t = sb("mask_t", (128, H * C), U8)
    idbf_t = sb("idbf_t", (128, 128), BF16)
    idf_t = sb("idf_t", (128, 128), F32)
    onc_t = sb("onc_t", (128, 1), BF16)
    onr_t = sb("onr_t", (1, 128), F32)
    cb_t = sb("cb_t", (128, 2), F32)
    nw_t = sb("nw_t", (128, NDI), F32)
    S_t = sb("S_t", (128, H * DH), F32)
    Sbf_t = sb("Sbf_t", (128, H * DH), BF16)
    xn_fm = sb("xn_fm", (128, NDI, GT), BF16)
    x_t = sb("x_t", (128, D), F32)

    nc.const_aps.register(F32, 1e-5, cb_t[:, 0:1])
    nc.const_aps.register(F32, 1e-6, cb_t[:, 1:2])

    psA = es.enter_context(nc.psum_tensor("psA", [128, 1024], F32))
    psC = es.enter_context(nc.psum_tensor("psC", [128, 1024], F32))
    psB = es.enter_context(nc.psum_tensor("psB", [128, 2048], F32))

    p = Prog(nc)
    for s in ("lw", "lx", "lg", "pe", "dv", "ac", "cc", "so"):
        p.sem(s)

    def dma(eng, out, in_, sem, waits=(), **kw):
        return p.emit(eng, lambda e, o=out, i=in_, k=kw: e.dma_start(out=o, in_=i, **k),
                      waits=waits, inc=sem, amt=16)

    W = lambda *pairs: [x for x in pairs if x is not None]

    dma("sync", mask_t[:], mask_in[:, :], "lw")
    dma("sync", idbf_t[:], idbf_in[:, :], "lw")
    dma("sync", idf_t[:], idf_in[:, :], "lw")
    dma("sync", onc_t[:], onc_in[:, :], "lw")
    dma("sync", onr_t[:], onr_in[:, :], "lw")
    dma("sync", cb_t[:], cb_in[:, :], "lw")
    dma("sync", nw_t[:], nw_in[0, :].rearrange("(a p) -> p a", p=128), "lw",
        allow_slow_non_contiguous=True)
    consts_lw = p.val("lw")

    p.emit("vector", lambda e: e.memset(S_t[:], 0.0), inc="dv")
    p.emit("vector", lambda e: e.memset(Sbf_t[:], 0.0), inc="dv")

    for dst, src in ((wqb, wq_in), (wkb, wk_in), (wvb, wv_in),
                     (wg1b, wg1_in), (wg2b, wg2_in)):
        dma("gpsimd", dst[:, :], src[:, :], "lg")
    qkv_cast_lg = p.val("lg")
    dma("gpsimd", w1b[:, :], w1_in[:, :], "lg")
    dma("gpsimd", w3b[:, :], w3_in[:, :], "lg")
    w13_cast_lg = p.val("lg")

    es0 = ExitStack()
    w2f_t = es0.enter_context(nc.sbuf_tensor("w2f_t", [128, 2, HID], F32))
    w2b_t = es0.enter_context(nc.sbuf_tensor("w2b_t", [128, HID], BF16))
    p0_ac = 0
    for dt_i in range(2):
        lwv = dma("sync", w2f_t[:, 0, :], w2s_in[dt_i * 128:(dt_i + 1) * 128, :], "lw",
                  waits=W(("ac", p0_ac) if dt_i else None))
        p.wait("vector", W(("lw", lwv), ("ac", p0_ac) if dt_i else None))
        cur = 0
        for st in range(13):
            h = 1 << st
            view = lambda buf: w2f_t[:, buf, :].rearrange(
                "p (n two h) -> p n two h", two=2, h=h)
            a_v, b_v = view(cur)[:, :, 0, :], view(cur)[:, :, 1, :]
            ao, bo = view(1 - cur)[:, :, 0, :], view(1 - cur)[:, :, 1, :]
            p.emit("vector", lambda e, a=a_v, b=b_v, o=ao: e.tensor_add(o, a, b),
                   inc="dv")
            p.emit("vector", lambda e, a=a_v, b=b_v, o=bo: e.tensor_sub(o, a, b),
                   inc="dv")
            cur = 1 - cur
        dvv = p.val("dv")
        p0_ac = p.emit("scalar", lambda e, cur=cur: e.activation(
            out=w2b_t[:], in_=w2f_t[:, cur, :], func=AF.Copy, scale=FWHT_SCALE),
            waits=W(("dv", dvv), ("lg", p.val("lg")) if dt_i else None), inc="ac")
        dma("gpsimd", w2pb[dt_i * 128:(dt_i + 1) * 128, :], w2b_t[:], "lg",
            waits=W(("ac", p0_ac)))
    w2p_lg = p.val("lg")
    p.emit("gpsimd", lambda e: e.collective_compute(
        "AllGather", ALU.bypass, replica_groups=[list(range(NCORE))],
        ins=[w2pb[:, :]], outs=[GW2[:, :]]),
        waits=W(("lg", w2p_lg)), inc="cc")
    cc_w2 = p.val("cc")
    p0b_ac, p0b_lg = p0_ac, w2p_lg
    es0.close()

    es1 = ExitStack()
    sb1 = lambda n, shape, dt: es1.enter_context(nc.sbuf_tensor(n, list(shape), dt))
    wfold = sb1("wfold", (128, D), F32)
    wfoldo = sb1("wfoldo", (128, D), BF16)
    w1tr = sb1("w1tr", (128, 2, HID // 2), BF16)
    gt_asm = sb1("gt_asm", (128, 2, NHH // 2, 128), BF16)
    ctab = sb1("ctab", (64, 2, GT), F32)
    stab = sb1("stab", (64, 2, GT), F32)
    wq_h = sb1("wq_h", (128, 2, NDI, 128), BF16)
    wk_h = sb1("wk_h", (128, 2, NDI, 128), BF16)
    wv_h = sb1("wv_h", (128, 2, NDI, 128), BF16)
    rt0 = sb1("rt0", (64, GT), F32); rt1 = sb1("rt1", (64, GT), F32)
    rt2 = sb1("rt2", (64, GT), F32); rt3 = sb1("rt3", (64, GT), F32)
    qh_g = sb1("qh_g", (128, H, GT), BF16)
    kh_g = sb1("kh_g", (128, H, GT), BF16)
    vf_h = sb1("vf_h", (128, 2, GT), BF16)
    v_g = sb1("v_g", (128, G, H * DH), BF16)
    scDT = sb1("scDT", (128, H * C), BF16)
    khT = sb1("khT", (128, 2, 128), BF16)
    o_t = sb1("o_t", (128, H * C), F32)
    dump = sb1("dump", (128, D), F32)
    xn_tm = sb1("xn_tm", (128, 2, D), BF16)
    ln_s = sb1("ln_s", (128, 8), F32)

    p.emit("vector", lambda e: e.memset(dump[:], 0.0), inc="dv")
    p.emit("vector", lambda e: e.memset(scDT[:], 0.0), inc="dv")
    zeros_dv = p.val("dv")
    dma("gpsimd", GS[8 * H * DH: 9 * H * DH, :].rearrange("(a p) e -> p a e", p=128),
        dump[:, 0:16 * 128].rearrange("p (a e) -> p a e", a=16),
        "lg", waits=W(("dv", zeros_dv)))
    gszero_lg = p.val("lg")

    wfold_free_ac = 0
    wfoldo_store_lx = 0
    for i in range(NDI):
        lwv = dma("scalar", wfold[:], wo_in[i * 128:(i + 1) * 128, :], "lx",
                  waits=W(("ac", wfold_free_ac if i else p0b_ac),
                          ("lg", p0b_lg) if i == 0 else None))
        wfold_free_ac = p.emit("scalar", lambda e, i=i: e.activation(
            out=wfoldo[:], in_=wfold[:], func=AF.Copy, scale=nw_t[:, i:i + 1]),
            waits=W(("lx", lwv), ("lw", consts_lw),
                    ("lx", wfoldo_store_lx) if i else None), inc="ac")
        p.wait("scalar", W(("ac", wfold_free_ac)))
        dma("scalar", wob[i * 128:(i + 1) * 128, :], wfoldo[:], "lx")
        wfoldo_store_lx = p.val("lx")
    wob_lx = p.val("lx")

    w1tr_free = [0, 0]
    w1tr_first = [True, True]
    for mi, (srcb, dstT) in enumerate(((w1b, w1T), (w3b, w3T))):
        for dic in range(NDI):
            for half in range(2):
                wpar = (mi * NDI * 2 + dic * 2 + half) % 2
                hh0 = half * (NHH // 2)
                lwv = dma("sync", w1tr[:, wpar, :],
                          srcb[hh0 * 128:(hh0 + NHH // 2) * 128,
                               dic * 128:(dic + 1) * 128], "lw",
                          waits=W(("lg", w13_cast_lg), ("lw", w1tr_free[wpar]),
                                  ("ac", p0b_ac) if w1tr_first[wpar] else None,
                                  ("lg", p0b_lg) if w1tr_first[wpar] else None),
                          transpose=True)
                w1tr_first[wpar] = False
                dma("sync",
                    dstT[hh0:hh0 + NHH // 2, :, dic, :].rearrange("hhc p hh -> p hhc hh"),
                    w1tr[:, wpar, :].rearrange("p (hhc hh) -> p hhc hh", hh=128),
                    "lw", waits=W(("lw", lwv)))
                w1tr_free[wpar] = p.val("lw")
    w13T_lw = p.val("lw")

    gt_free = [0, 0]
    for dc in range(NDI):
        for half in range(2):
            gpar = (dc * 2 + half) % 2
            hh0 = half * (NHH // 2)
            for hhc in range(NHH // 2):
                dma("scalar", gt_asm[:, gpar, hhc, :],
                    GW2[dc * 128:(dc + 1) * 128,
                        (hh0 + hhc) * 128:(hh0 + hhc + 1) * 128],
                    "lx", waits=W(("cc", cc_w2) if (dc == 0 and half == 0) else None,
                                  ("lw", w13T_lw) if (dc == 0 and half == 0) else None,
                                  ("lx", gt_free[gpar]) if hhc == 0 else None),
                    transpose=True)
            lxv = p.val("lx")
            dma("scalar", GT2[dc, :, hh0:hh0 + NHH // 2, :]
                .rearrange("p hhc hh -> p hhc hh"),
                gt_asm[:, gpar, :, :], "lx", waits=W(("lx", lxv)))
            gt_free[gpar] = p.val("lx")
    gt2_lx = p.val("lx")

    ln_done_lg = [0] * NCH
    o_done_lg = [0] * NCH

    xsum = ln_s[:, 0:1]; xsq = ln_s[:, 1:2]; m_c = ln_s[:, 2:3]
    msq = ln_s[:, 3:4]; var_c = ln_s[:, 4:5]; sd_c = ln_s[:, 5:6]
    rstd_c = ln_s[:, 6:7]; negmr = ln_s[:, 7:8]

    xt_free = None
    xntm_store = [0, 0]
    wbuf_free_pe = {}
    psA_free = [("dv", 0), ("dv", 0)]
    psC_free = [("dv", 0), ("dv", 0)]
    tab_free = [0, 0]
    xnfm_last_pe = 0
    qh_store_lg = [0] * NG

    for g in range(NG):
        for jj in range(G):
            ch = g * G + jj
            par = ch % 2
            lxv = dma("scalar", x_t[:], x_in[ch * 128:(ch + 1) * 128, :], "lx",
                      waits=W(xt_free))
            p.emit("vector", lambda e: e.reduce_sum(xsum, x_t[:], axis=mybir.AxisListType.X),
                   waits=W(("lx", lxv)), inc="dv")
            acv = p.emit("scalar", lambda e: e.activation(
                out=dump[:], in_=x_t[:], func=AF.Square, accum_out=xsq),
                waits=W(("lx", lxv), ("lg", gszero_lg) if ch == 0 else None), inc="ac")
            p.emit("vector", lambda e: e.tensor_scalar(m_c, xsum, 1.0 / D, None, ALU.mult),
                   inc="dv")
            p.emit("vector", lambda e: e.tensor_scalar(msq, xsq, 1.0 / D, None, ALU.mult),
                   waits=W(("ac", acv)), inc="dv")
            p.emit("vector", lambda e: e.tensor_mul(var_c, m_c, m_c), inc="dv")
            dvv = p.emit("vector", lambda e: e.tensor_sub(var_c, msq, var_c), inc="dv")
            acv = p.emit("scalar", lambda e: e.activation(
                out=sd_c, in_=var_c, func=AF.Sqrt, bias=1e-5),
                waits=W(("dv", dvv)), inc="ac")
            p.emit("vector", lambda e: e.reciprocal(rstd_c, sd_c),
                   waits=W(("ac", acv)), inc="dv")
            p.emit("vector", lambda e: e.tensor_scalar(
                negmr, m_c, -1.0, None, ALU.mult), inc="dv")
            dvv = p.emit("vector", lambda e, par=par: e.tensor_scalar(
                xn_tm[:, par, :], x_t[:], negmr, rstd_c, ALU.add, ALU.mult),
                waits=W(("lg", xntm_store[par])), inc="dv")
            xt_free = ("dv", dvv)
            dma("gpsimd", xn_scr[ch * 128:(ch + 1) * 128, :], xn_tm[:, par, :], "lg",
                waits=W(("dv", dvv)))
            xntm_store[par] = p.val("lg")
            ln_done_lg[ch] = p.val("lg")

        for di in range(NDI):
            dma("sync", xn_fm[:, di, :],
                xn_scr[g * GT:(g + 1) * GT, di * 128:(di + 1) * 128], "lw",
                waits=W(("lg", ln_done_lg[g * G + G - 1]), ("pe", xnfm_last_pe))
                if di == 0 else (), transpose=True)
        xn_lw = p.val("lw")

        for which, w_b, w_tiles, tab_q in (
                ("q", wqb, wq_h, (cosq_in, sinq_in)),
                ("k", wkb, wk_h, (cosk_in, sink_in)),
                ("v", wvb, wv_h, None)):
            for h in range(H):
                par = h % 2
                if tab_q is not None:
                    first_tab = (g == 0 and which == "q" and h < 2)
                    dma("scalar", ctab[:, par, :], tab_q[0][g, h, :, :], "lx",
                        waits=W(("dv", tab_free[par]),
                                ("ac", p0b_ac) if first_tab else None,
                                ("lg", p0b_lg) if first_tab else None))
                    dma("scalar", stab[:, par, :], tab_q[1][g, h, :, :], "lx")
                    tab_lx = p.val("lx")
                ps = psA[:, par * 512:(par + 1) * 512]
                first_w = (g == 0 and h < 2)
                lwv = dma("sync", w_tiles[:, par],
                          w_b[:, h * 128:(h + 1) * 128]
                          .rearrange("(dic p) o -> p dic o", p=128), "lw",
                          waits=W(("pe", wbuf_free_pe.get((which, par), 0)),
                                  ("lg", p0b_lg) if first_w else None,
                                  ("ac", p0b_ac) if first_w else None))
                for di in range(NDI):
                    wts = ()
                    if di == 0:
                        wts = W(("lw", lwv), ("lw", xn_lw), psA_free[par],
                                ("lw", consts_lw) if (g == 0 and which == "q" and h == 0) else None)
                    pev = p.emit("tensor", lambda e, di=di, par=par, ps=ps, wt=w_tiles:
                                 e.matmul(ps, wt[:, par, di, :], xn_fm[:, di, :],
                                          start=(di == 0), stop=(di == NDI - 1)),
                                 waits=wts, inc="pe" if di == NDI - 1 else None)
                wbuf_free_pe[(which, par)] = pev
                if which == "v":
                    dvv = p.emit("vector", lambda e, par=par, ps=ps: e.tensor_copy(
                        vf_h[:, par, :], ps), waits=W(("pe", pev)), inc="dv")
                    psA_free[par] = ("dv", dvv)
                    for tch in range(G):
                        cpar = tch % 2
                        pev2 = p.emit("tensor", lambda e, par=par, tch=tch, cpar=cpar:
                                      e.matmul(psC[:, cpar * 512 + 0:cpar * 512 + 128],
                                               vf_h[:, par, tch * 128:(tch + 1) * 128],
                                               idbf_t[:], start=True, stop=True),
                                      waits=W(("dv", dvv), psC_free[cpar]), inc="pe")
                        dvv2 = p.emit("vector", lambda e, h=h, tch=tch, cpar=cpar:
                                      e.tensor_copy(v_g[:, tch, h * 128:(h + 1) * 128],
                                                    psC[:, cpar * 512:cpar * 512 + 128]),
                                      waits=W(("pe", pev2)), inc="dv")
                        psC_free[cpar] = ("dv", dvv2)
                else:
                    acv = p.emit("scalar", lambda e, ps=ps: e.activation(
                        out=ps, in_=ps, func=AF.Silu), waits=W(("pe", pev)), inc="ac")
                    dst = qh_g if which == "q" else kh_g
                    q1 = ps[0:64, :]; q2 = ps[64:128, :]
                    p.emit("vector", lambda e, q1=q1, par=par: e.tensor_mul(
                        rt0[:], q1, ctab[:, par, :]),
                        waits=W(("ac", acv), ("lx", tab_lx)), inc="dv")
                    p.emit("vector", lambda e, q2=q2, par=par: e.tensor_mul(
                        rt1[:], q2, stab[:, par, :]), inc="dv")
                    p.emit("vector", lambda e, q1=q1, par=par: e.tensor_mul(
                        rt2[:], q1, stab[:, par, :]), inc="dv")
                    dvv = p.emit("vector", lambda e, q2=q2, par=par: e.tensor_mul(
                        rt3[:], q2, ctab[:, par, :]), inc="dv")
                    psA_free[par] = ("dv", dvv)
                    tab_free[par] = dvv
                    p.emit("vector", lambda e, d=dst, h=h: e.tensor_sub(
                        d[0:64, h, :], rt0[:], rt1[:]), inc="dv")
                    dvv = p.emit("vector", lambda e, d=dst, h=h: e.tensor_add(
                        d[64:128, h, :], rt2[:], rt3[:]), inc="dv")
        rot_dv = p.val("dv")

        dma("gpsimd", qt_scr[g, :, :, :], qh_g[:, :, :], "lg", waits=W(("dv", rot_dv)))
        qh_store_lg[g] = p.val("lg")

        for jj in range(G):
            ch = g * G + jj
            t0, t1 = jj * 128, (jj + 1) * 128
            for h in range(H):
                wts = W(("dv", p.val("dv"))) if h == 0 else ()
                pev = p.emit("tensor", lambda e, h=h, t0=t0, t1=t1: e.matmul(
                    psB[:, h * 128:(h + 1) * 128],
                    kh_g[:, h, t0:t1], qh_g[:, h, t0:t1], start=True, stop=True),
                    waits=wts, inc="pe" if h == H - 1 else None)
            scv = p.emit("vector", lambda e: e.copy_predicated(scDT[:], mask_t[:], psB[:]),
                         waits=W(("pe", pev)), inc="dv")
            for h in range(H):
                wts = W(("dv", scv)) if h == 0 else ()
                p.emit("tensor", lambda e, h=h, jj=jj: e.matmul(
                    psB[:, h * 128:(h + 1) * 128],
                    v_g[:, jj, h * 128:(h + 1) * 128],
                    scDT[:, h * 128:(h + 1) * 128], start=True, stop=False), waits=wts)
                pev = p.emit("tensor", lambda e, h=h, t0=t0, t1=t1: e.matmul(
                    psB[:, h * 128:(h + 1) * 128],
                    Sbf_t[:, h * 128:(h + 1) * 128],
                    qh_g[:, h, t0:t1], start=False, stop=True),
                    inc="pe" if h == H - 1 else None)
            oev = p.emit("vector", lambda e: e.tensor_copy(o_t[:], psB[:]),
                         waits=W(("pe", pev),
                                 ("lg", o_done_lg[ch - 1] if ch else 0)), inc="dv")
            dma("gpsimd", o_scr[ch, :, :], o_t[:], "lg", waits=W(("dv", oev)))
            o_done_lg[ch] = p.val("lg")
            for h in range(H):
                par = h % 2
                pev = p.emit("tensor", lambda e, h=h, par=par, t0=t0, t1=t1: e.matmul(
                    psC[:, par * 512:par * 512 + 128],
                    kh_g[:, h, t0:t1], idbf_t[:], start=True, stop=True),
                    waits=W(psC_free[par], ("dv", oev) if h == 0 else None), inc="pe")
                dvv = p.emit("vector", lambda e, par=par: e.tensor_copy(
                    khT[:, par, :], psC[:, par * 512:par * 512 + 128]),
                    waits=W(("pe", pev)), inc="dv")
                psC_free[par] = ("dv", dvv)
                p.emit("tensor", lambda e, h=h, par=par, jj=jj: e.matmul(
                    psB[:, h * 128:(h + 1) * 128], khT[:, par, :],
                    v_g[:, jj, h * 128:(h + 1) * 128], start=True, stop=True),
                    waits=W(("dv", dvv)), inc="pe" if h == H - 1 else None)
            pev = p.val("pe")
            p.emit("vector", lambda e: e.tensor_add(S_t[:], S_t[:], psB[:]),
                   waits=W(("pe", pev)), inc="dv")
            for h in range(H):
                p.emit("vector", lambda e, h=h: e.tensor_scalar(
                    S_t[:, h * 128:(h + 1) * 128], S_t[:, h * 128:(h + 1) * 128],
                    float(LAM_C[h]), None, ALU.mult), inc="dv")
            p.emit("vector", lambda e: e.tensor_copy(Sbf_t[:], S_t[:]), inc="dv")
        xnfm_last_pe = p.val("pe")

    sdv = p.val("dv")
    for h in range(H):
        dma("gpsimd", Sout[h * 128:(h + 1) * 128, :], S_t[:, h * 128:(h + 1) * 128],
            "lg", waits=W(("dv", sdv)) if h == 0 else ())
    s_lg = p.val("lg")
    p.emit("gpsimd", lambda e: e.collective_compute(
        "AllGather", ALU.bypass, replica_groups=[list(range(NCORE))],
        ins=[Sout[:, :]], outs=[GS[0:8 * H * DH, :]]),
        waits=W(("lg", s_lg)), inc="cc")
    cc_s = p.val("cc")
    p1_lg = p.val("lg")
    p1_lx = p.val("lx")

    p.barrier()
    es1.close()

    es2 = ExitStack()
    sb2 = lambda n, shape, dt: es2.enter_context(nc.sbuf_tensor(n, list(shape), dt))
    w1_l = sb2("w1_l", (128, 2, NDI, 128), BF16)
    w3_l = sb2("w3_l", (128, 2, NDI, 128), BF16)
    hid_t = sb2("hid_t", (128, NHH, GT), BF16)
    sil_t = sb2("sil_t", (128, 2, GT), F32)
    gw_l = sb2("gw_l", (128, 2, NHH, 128), BF16)
    g_st = sb2("g_st", (128, 2, GT), F32)

    UL_SLOTS = [psA[:, 0:512], psA[:, 512:1024], psC[:, 0:512], psC[:, 512:1024]]
    ul_free = [("dv", 0)] * 4
    w1l_free = [0, 0]
    gw_free = [0, 0]
    gB_free = [("ac", 0), ("ac", 0)]
    gst_free = [0, 0]
    xnfm_last_pe = 0

    for stl in range(NG):
        tb = slice(stl * GT, (stl + 1) * GT)
        for di in range(NDI):
            dma("sync", xn_fm[:, di, :], xn_scr[tb, di * 128:(di + 1) * 128], "lw",
                waits=W(("pe", xnfm_last_pe), ("lw", w13T_lw)) if di == 0 else (),
                transpose=True)
        xn_lw = p.val("lw")
        for hhc in range(NHH):
            par = hhc % 2
            su = (2 * hhc) % 4
            sl = (2 * hhc + 1) % 4
            lwv1 = dma("sync", w1_l[:, par], w1T[hhc, :, :, :].rearrange("p dic hh -> p dic hh"), "lw",
                       waits=W(("pe", w1l_free[par]),
                               ("lg", p1_lg) if (stl == 0 and hhc < 2) else None,
                               ("lx", p1_lx) if (stl == 0 and hhc < 2) else None))
            lwv3 = dma("sync", w3_l[:, par], w3T[hhc, :, :, :].rearrange("p dic hh -> p dic hh"), "lw")
            for di in range(NDI):
                wts = W(("lw", lwv3), ("lw", xn_lw), ul_free[su]) if di == 0 else ()
                p.emit("tensor", lambda e, di=di, par=par, ps=UL_SLOTS[su]: e.matmul(
                    ps, w1_l[:, par, di, :], xn_fm[:, di, :],
                    start=(di == 0), stop=(di == NDI - 1)), waits=wts)
            for di in range(NDI):
                wts = W(ul_free[sl]) if di == 0 else ()
                pev = p.emit("tensor", lambda e, di=di, par=par, ps=UL_SLOTS[sl]: e.matmul(
                    ps, w3_l[:, par, di, :], xn_fm[:, di, :],
                    start=(di == 0), stop=(di == NDI - 1)), waits=wts,
                    inc="pe" if di == NDI - 1 else None)
            w1l_free[par] = pev
            acv = p.emit("scalar", lambda e, ps=UL_SLOTS[su], par=par: e.activation(
                out=sil_t[:, par, :], in_=ps, func=AF.Silu), waits=W(("pe", pev)), inc="ac")
            dvv = p.emit("vector", lambda e, hhc=hhc, ps=UL_SLOTS[sl], par=par:
                         e.tensor_mul(hid_t[:, hhc, :], sil_t[:, par, :], ps),
                         waits=W(("ac", acv)), inc="dv")
            ul_free[su] = ("ac", acv)
            ul_free[sl] = ("dv", dvv)
        hid_dv = p.val("dv")
        for dc in range(NDI):
            par = dc % 2
            lwv = dma("sync", gw_l[:, par], GT2[dc, :, :, :].rearrange("p hhc do -> p hhc do"), "lw",
                      waits=W(("pe", gw_free[par]),
                              ("lx", gt2_lx) if (stl == 0 and dc < 2) else None,
                              ("lg", p1_lg) if (stl == 0 and dc < 2) else None))
            psg = psB[:, par * 512:(par + 1) * 512]
            for hhc in range(NHH):
                wts = W(("lw", lwv), ("dv", hid_dv), gB_free[par]) if hhc == 0 else ()
                pev = p.emit("tensor", lambda e, hhc=hhc, par=par, ps=psg: e.matmul(
                    ps, gw_l[:, par, hhc, :], hid_t[:, hhc, :],
                    start=(hhc == 0), stop=(hhc == NHH - 1)), waits=wts,
                    inc="pe" if hhc == NHH - 1 else None)
            gw_free[par] = pev
            acv = p.emit("scalar", lambda e, ps=psg, par=par: e.activation(
                out=g_st[:, par, :], in_=ps, func=AF.Silu),
                waits=W(("pe", pev), ("lx", gst_free[par])), inc="ac")
            gB_free[par] = ("ac", acv)
            dma("scalar", g_scr[stl, dc, :, :], g_st[:, par, :], "lx", waits=W(("ac", acv)))
            gst_free[par] = p.val("lx")
        xnfm_last_pe = p.val("pe")
    g_done_lx = p.val("lx")
    p2b_lg = p.val("lg")

    p.barrier()
    es2.close()

    es3 = ExitStack()
    sb3 = lambda n, shape, dt: es3.enter_context(nc.sbuf_tensor(n, list(shape), dt))
    qh_ld = sb3("qh_ld", (128, H, GT), BF16)
    o_ld = sb3("o_ld", (128, 2, H * C), F32)
    sg_t = sb3("sg_t", (128, H, GT), F32)
    on_sub = sb3("on_sub", (128, H, GT), BF16)
    osq_t = sb3("osq_t", (128, H * C), BF16)
    msr = sb3("msr", (1, H * C), F32)
    rsr = sb3("rsr", (1, H * C), F32)
    gb_t = sb3("gb_t", (128, GT), BF16)
    wg1_t = sb3("wg1_t", (128, NDI, 128), BF16)
    wg2_t = sb3("wg2_t", (128, D), BF16)
    wob_c = sb3("wob_c", (128, 2, NDI, 128), BF16)
    g_ld = sb3("g_ld", (128, 2, GT), F32)
    sgf = sb3("sgf", (128, GT), F32)
    out_t = sb3("out_t", (128, G, D), F32)

    def load_sin(e):
        off = e.value_load(sgsel_in[0:1, 0:1])
        return e.dma_start(out=o_ld[:, 0, :].rearrange("p (h e) -> p h e", h=H),
                           in_=GS[bass.ds(off * (H * DH), H * DH), :]
                           .rearrange("(h d) e -> d h e", d=128))
    p.emit("gpsimd", load_sin, waits=W(("cc", cc_s), ("lx", g_done_lx)),
           inc="lg", amt=16)
    sin_lg = p.val("lg")
    sin_dv = p.emit("vector", lambda e: e.tensor_copy(Sbf_t[:], o_ld[:, 0, :]),
                    waits=W(("lg", sin_lg)), inc="dv")

    dma("sync", wg1_t[:, :, :], wg1b[:, :].rearrange("(dic p) o -> p dic o", p=128),
        "lw", waits=W(("lx", g_done_lx), ("lg", p2b_lg)))
    dma("sync", wg2_t[:], wg2b[:, :], "lw")
    wg_lw = p.val("lw")

    out_store_so = [0] * G
    qh_free_pe = 0
    old_free = [("dv", sin_dv), ("dv", 0)]
    wob_free = [0, 0]
    gld_free = [("dv", 0), ("dv", 0)]
    xnfm_last_pe = p.val("pe")
    psA_free2 = [("dv", 0), ("dv", 0)]
    psC_free2 = [("dv", 0), ("dv", 0)]
    sB_free = [("dv", 0), ("dv", 0)]

    for stl in range(NG):
        lxq = dma("scalar", qh_ld[:, :, :], qt_scr[stl, :, :, :], "lx",
                  waits=W(("pe", qh_free_pe),
                          ("lx", g_done_lx) if stl == 0 else None))
        for di in range(NDI):
            dma("sync", xn_fm[:, di, :],
                xn_scr[stl * GT:(stl + 1) * GT, di * 128:(di + 1) * 128], "lw",
                waits=W(("pe", xnfm_last_pe), ("lw", wg_lw)) if di == 0 else (),
                transpose=True)
        xn_lw = p.val("lw")
        for di in range(NDI):
            wts = W(("lw", xn_lw), psA_free2[0], ("lx", lxq)) if di == 0 else ()
            pev = p.emit("tensor", lambda e, di=di: e.matmul(
                psA[:, 0:512], wg1_t[:, di, :], xn_fm[:, di, :],
                start=(di == 0), stop=(di == NDI - 1)), waits=wts,
                inc="pe" if di == NDI - 1 else None)
        dvv = p.emit("vector", lambda e: e.tensor_copy(gb_t[:], psA[:, 0:512]),
                     waits=W(("pe", pev)), inc="dv")
        psA_free2[0] = ("dv", dvv)
        for h in range(H):
            sl = h % 2
            pev = p.emit("tensor", lambda e, h=h, sl=sl: e.matmul(
                psA[:, sl * 512:(sl + 1) * 512], wg2_t[:, h * 128:(h + 1) * 128],
                gb_t[:], start=True, stop=True),
                waits=W(("dv", dvv) if h == 0 else None, psA_free2[sl]), inc="pe")
            acv = p.emit("scalar", lambda e, h=h, sl=sl: e.activation(
                out=sg_t[:, h, :], in_=psA[:, sl * 512:(sl + 1) * 512],
                func=AF.Sigmoid), waits=W(("pe", pev)), inc="ac")
            psA_free2[sl] = ("ac", acv)
        sig_ac = p.val("ac")

        for jj in range(G):
            ch = stl * G + jj
            par = jj % 2
            lxo = dma("scalar", o_ld[:, par, :], o_scr[ch, :, :], "lx",
                      waits=W(old_free[par],
                              ("lx", g_done_lx) if (stl == 0 and jj < 2) else None))
            for h in range(H):
                wts = W(("lx", lxq), ("dv", sin_dv), ("lx", lxo), sB_free[0],
                        sB_free[1]) if h == 0 else ()
                pev = p.emit("tensor", lambda e, h=h, jj=jj: e.matmul(
                    psB[:, h * 128:(h + 1) * 128], Sbf_t[:, h * 128:(h + 1) * 128],
                    qh_ld[:, h, jj * 128:(jj + 1) * 128], start=True, stop=True),
                    waits=wts, inc="pe" if h == H - 1 else None)
            for h in range(H):
                lam_j = LAM_C[h] ** ch
                scale = float(np.float32(lam_j)) if lam_j > 1e-44 else 0.0
                p.emit("vector", lambda e, h=h, par=par, s=scale: e.scalar_tensor_tensor(
                    o_ld[:, par, h * 128:(h + 1) * 128], psB[:, h * 128:(h + 1) * 128],
                    s, o_ld[:, par, h * 128:(h + 1) * 128], ALU.mult, ALU.add),
                    waits=W(("pe", pev), ("ac", sig_ac)) if h == 0 else (), inc="dv")
            dvv = p.emit("vector", lambda e, par=par, jj=jj: e.tensor_mul(
                o_ld[:, par, :].rearrange("p (h t) -> p h t", h=H),
                o_ld[:, par, :].rearrange("p (h t) -> p h t", h=H),
                sg_t[:, :, jj * 128:(jj + 1) * 128]), inc="dv")
            sB_free[0] = ("dv", dvv); sB_free[1] = ("dv", dvv)
            acv = p.emit("scalar", lambda e, par=par: e.activation(
                out=osq_t[:], in_=o_ld[:, par, :], func=AF.Square),
                waits=W(("dv", dvv)), inc="ac")
            for qd in range(4):
                sl = qd % 2
                pev = p.emit("tensor", lambda e, qd=qd, sl=sl: e.matmul(
                    psA[0:1, sl * 512:(sl + 1) * 512], onc_t[:],
                    osq_t[:, qd * 512:(qd + 1) * 512], start=True, stop=True),
                    waits=W(("ac", acv) if qd == 0 else None, psA_free2[sl]), inc="pe")
                dvv = p.emit("vector", lambda e, qd=qd, sl=sl: e.tensor_copy(
                    msr[:, qd * 512:(qd + 1) * 512], psA[0:1, sl * 512:(sl + 1) * 512]),
                    waits=W(("pe", pev)), inc="dv")
                psA_free2[sl] = ("dv", dvv)
            p.emit("scalar", lambda e: e.activation(
                out=rsr[:], in_=msr[:], func=AF.Ln, scale=1.0 / DH, bias=1e-6),
                waits=W(("dv", dvv)), inc="ac")
            acv = p.emit("scalar", lambda e: e.activation(
                out=rsr[:], in_=rsr[:], func=AF.Exp, scale=-0.5), inc="ac")
            for qd in range(4):
                sl = qd % 2
                pev = p.emit("tensor", lambda e, qd=qd, sl=sl: e.matmul(
                    psA[:, sl * 512:(sl + 1) * 512], onr_t[:],
                    rsr[:, qd * 512:(qd + 1) * 512], start=True, stop=True),
                    waits=W(("ac", acv) if qd == 0 else None, psA_free2[sl]), inc="pe")
                dvv = p.emit("vector", lambda e, qd=qd, sl=sl, par=par, jj=jj:
                             e.tensor_mul(
                                 on_sub[:, qd * 4:(qd + 1) * 4, jj * 128:(jj + 1) * 128],
                                 o_ld[:, par, qd * 512:(qd + 1) * 512]
                                 .rearrange("p (h t) -> p h t", h=4),
                                 psA[:, sl * 512:(sl + 1) * 512]
                                 .rearrange("p (h t) -> p h t", h=4)),
                             waits=W(("pe", pev)), inc="dv")
                psA_free2[sl] = ("dv", dvv)
            old_free[par] = ("dv", dvv)
        on_dv = p.val("dv")

        for dc in range(NDI):
            par = dc % 2
            lwv = dma("sync", wob_c[:, par], wob[:, dc * 128:(dc + 1) * 128]
                      .rearrange("(dic p) o -> p dic o", p=128), "lw",
                      waits=W(("pe", wob_free[par]),
                              ("lx", g_done_lx) if (stl == 0 and dc < 2) else None))
            lxg = dma("scalar", g_ld[:, par, :], g_scr[stl, dc, :, :], "lx",
                      waits=W(gld_free[par], ("lx", g_done_lx) if stl == 0 and dc < 2 else None))
            psg = psB[:, par * 512:(par + 1) * 512]
            for h in range(H):
                wts = W(("lw", lwv), ("dv", on_dv), sB_free[par]) if h == 0 else ()
                pev = p.emit("tensor", lambda e, h=h, par=par, ps=psg: e.matmul(
                    ps, wob_c[:, par, h, :], on_sub[:, h, :],
                    start=(h == 0), stop=(h == H - 1)), waits=wts,
                    inc="pe" if h == H - 1 else None)
            wob_free[par] = pev
            dvv = p.emit("vector", lambda e, ps=psg, par=par: e.tensor_mul(
                sgf[:], ps, g_ld[:, par, :]), waits=W(("pe", pev), ("lx", lxg)), inc="dv")
            gld_free[par] = ("dv", dvv)
            sB_free[par] = ("dv", dvv)
            for tb_i in range(G):
                cpar = tb_i % 2
                pev = p.emit("tensor", lambda e, tb_i=tb_i, cpar=cpar: e.matmul(
                    psC[:, cpar * 512:cpar * 512 + 128],
                    sgf[:, tb_i * 128:(tb_i + 1) * 128], idf_t[:],
                    start=True, stop=True),
                    waits=W(("dv", dvv) if tb_i == 0 else None, psC_free2[cpar]),
                    inc="pe")
                dvv2 = p.emit("vector", lambda e, tb_i=tb_i, cpar=cpar, dc=dc:
                              e.tensor_copy(out_t[:, tb_i, dc * 128:(dc + 1) * 128],
                                            psC[:, cpar * 512:cpar * 512 + 128]),
                              waits=W(("pe", pev),
                                      ("so", out_store_so[tb_i]) if dc == 0 else None),
                              inc="dv")
                psC_free2[cpar] = ("dv", dvv2)
        asm_dv = p.val("dv")
        for tb_i in range(G):
            ch = stl * G + tb_i
            lxx = dma("scalar", x_t[:], x_in[ch * 128:(ch + 1) * 128, :], "lx",
                      waits=W(xt_free))
            dvv = p.emit("vector", lambda e, tb_i=tb_i: e.tensor_add(
                out_t[:, tb_i, :], out_t[:, tb_i, :], x_t[:]),
                waits=W(("lx", lxx), ("dv", asm_dv)), inc="dv")
            xt_free = ("dv", dvv)
            dma("gpsimd", y_out[ch * 128:(ch + 1) * 128, :], out_t[:, tb_i, :], "so",
                waits=W(("dv", dvv)))
            out_store_so[tb_i] = p.val("so")
        qh_free_pe = p.val("pe")
        xnfm_last_pe = p.val("pe")

    p.wait("gpsimd", W(("so", p.val("so"))))

    if taps:
        tapmap = {"xn": xn_scr, "qt": qt_scr, "o": o_scr, "g": g_scr,
                  "w2p": GW2, "gs": GS, "w1T": w1T, "gt2": GT2, "sc": sc_dbg}
        for t in taps:
            src = tapmap[t]
            tp = nc.dram_tensor("tap_" + t, list(src.tensor.shape), src.tensor.dtype,
                                kind="ExternalOutput").ap()
            idx = tuple([slice(None)] * len(src.tensor.shape))
            dma("gpsimd", tp[idx], src[idx], "so")
        p.wait("gpsimd", W(("so", p.val("so"))))

    with nc.Block() as block:
        p.run(block)
    es3.close()
    es.close()
    return nc


def kernel(L0_attn_x, L0_attn_Wq, L0_attn_Wk, L0_attn_Wv, L0_attn_Wo,
           L0_attn_Wg1, L0_attn_Wg2, L0_attn_norm_weight,
           L0_ffn_W1, L0_ffn_W2, L0_ffn_W3):
    x = np.ascontiguousarray(np.asarray(L0_attn_x, dtype=np.float32))
    weights = {
        "Wq": np.ascontiguousarray(L0_attn_Wq, dtype=np.float32),
        "Wk": np.ascontiguousarray(L0_attn_Wk, dtype=np.float32),
        "Wv": np.ascontiguousarray(L0_attn_Wv, dtype=np.float32),
        "Wo": np.ascontiguousarray(L0_attn_Wo, dtype=np.float32),
        "Wg1": np.ascontiguousarray(L0_attn_Wg1, dtype=np.float32),
        "Wg2": np.ascontiguousarray(L0_attn_Wg2, dtype=np.float32),
        "nw": np.ascontiguousarray(np.asarray(L0_attn_norm_weight,
                                              dtype=np.float32)[None, :]),
        "W1": np.ascontiguousarray(L0_ffn_W1, dtype=np.float32),
        "W3": np.ascontiguousarray(L0_ffn_W3, dtype=np.float32),
    }
    W2 = np.ascontiguousarray(L0_ffn_W2, dtype=np.float32)
    nc = build_nc(DEBUG_TAPS)
    in_maps = []
    for c in range(NCORE):
        b = c // 2
        t0 = (c % 2) * TC
        im = {"x_core": np.ascontiguousarray(x[b, t0:t0 + TC, :])}
        im.update(weights)
        im["W2s"] = np.ascontiguousarray(
            W2[c * (D // NCORE):(c + 1) * (D // NCORE), :])
        im.update(host_constants(c))
        in_maps.append(im)
    res = run_bass_kernel_spmd(nc, in_maps, core_ids=list(range(NCORE)))
    out = np.empty((B, N, D), dtype=np.float32)
    for c in range(NCORE):
        b = c // 2
        t0 = (c % 2) * TC
        out[b, t0:t0 + TC, :] = res.results[c]["y_core"]
    if DEBUG_TAPS:
        kernel.debug = res.results
    return out
